# version 01 (final)
# baseline (speedup 1.0000x reference)
import numpy as np
import ml_dtypes
from contextlib import ExitStack

import concourse.bass as bass
import concourse.mybir as mybir
from concourse.bass_utils import run_bass_kernel_spmd
from concourse.library_config import mlp as _mlp_lib

F32 = mybir.dt.float32
BF16 = mybir.dt.bfloat16
I16 = mybir.dt.int16
I32 = mybir.dt.int32
U32 = mybir.dt.uint32
U8 = mybir.dt.uint8
AF = mybir.ActivationFunctionType
OP = mybir.AluOpType
AX = mybir.AxisListType

T, D, H, DH, FF, E, CAP = 2048, 1024, 16, 64, 4096, 8, 1024
NT, DC, FC = T // 128, D // 128, FF // 128
POOL_ROWS = 12416


class Sem:
    def __init__(self, h):
        self.h, self.n = h, 0

    def inc(self, instr, k=1):
        self.n += k
        instr.then_inc(self.h, k)
        return self.n


def build_nc():
    nc = bass.Bass("TRN2", target_bir_lowering=False)

    x_d = nc.dram_tensor("x", [T, D], F32, kind="ExternalInput").ap()
    wqkvT_d = nc.dram_tensor("wqkvT", [D, 3 * D], F32, kind="ExternalInput").ap()
    bqkv_d = nc.dram_tensor("bqkv_cols", [64, 3 * H], F32, kind="ExternalInput").ap()
    qn_d = nc.dram_tensor("qn_col", [64, 1], F32, kind="ExternalInput").ap()
    kn_d = nc.dram_tensor("kn_col", [64, 1], F32, kind="ExternalInput").ap()
    wpT_d = nc.dram_tensor("wpT", [D, D], F32, kind="ExternalInput").ap()
    bp_d = nc.dram_tensor("bp_row", [1, D], F32, kind="ExternalInput").ap()
    wg_d = nc.dram_tensor("wg", [D, E], F32, kind="ExternalInput").ap()
    w1_d = nc.dram_tensor("w1", [E, D, FF], F32, kind="ExternalInput").ap()
    b1_d = nc.dram_tensor("b1_cols", [E, 128, FC], F32, kind="ExternalInput").ap()
    w2_d = nc.dram_tensor("w2", [E, FF, D], F32, kind="ExternalInput").ap()
    b2_d = nc.dram_tensor("b2", [E, D], F32, kind="ExternalInput").ap()
    out_d = nc.dram_tensor("out", [T, D], F32, kind="ExternalOutput").ap()

    ident_f_c = nc.inline_tensor(np.eye(128, dtype=np.float32), name="ident_f_c").ap()
    ident_b_c = nc.inline_tensor(np.eye(128, dtype=ml_dtypes.bfloat16), name="ident_b_c").ap()
    tri_np = (np.arange(128)[:, None] < np.arange(128)[None, :]).astype(np.float32)
    tri_c = nc.inline_tensor(tri_np, name="tri_c").ap()
    ones2d_c = nc.inline_tensor(np.ones((128, 128), np.float32), name="ones2d_c").ap()
    iota8_c = nc.inline_tensor(np.broadcast_to(np.arange(8, dtype=np.float32), (128, 8)).copy(), name="iota8_c").ap()
    tcol_c = nc.inline_tensor((np.arange(NT)[None, :] * 128 + np.arange(128)[:, None]).astype(np.float32), name="tcol_c").ap()
    pay_np = np.zeros((128, NT, 16), np.int16)
    ii = np.arange(T)
    pay_np[ii % 128, ii // 128, 0] = ((ii % 16) * 128 + ii // 16).astype(np.int16)
    pay1_c = nc.inline_tensor(pay_np, name="pay1_c").ap()
    pay2_c = nc.inline_tensor(pay_np.copy(), name="pay2_c").ap()
    idn_np = np.zeros((128, T // 16), np.int16)
    for rep in range(8):
        idn_np[rep * 16 + ii % 16, ii // 16] = ii.astype(np.int16)
    idident_c = nc.inline_tensor(idn_np, name="idident_c").ap()

    y_pool = nc.dram_tensor("y_pool", [T, D], BF16, kind="Internal").ap()
    tok_pool = nc.dram_tensor("tok_pool", [POOL_ROWS, 128], I16, kind="Internal").ap()
    ye_pool = nc.dram_tensor("ye_pool", [E * CAP + 128, D], BF16, kind="Internal").ap()
    tokinfo = nc.dram_tensor("tokinfo", [2, T], I16, kind="Internal").ap()

    st = ExitStack()
    with st:
        def sb(name, shape, dt):
            return st.enter_context(nc.sbuf_tensor(name, shape, dt)).ap()

        def mkpsum(name, shape, dt=F32):
            return st.enter_context(nc.psum_tensor(name, shape, dt)).ap()

        def mksem(name):
            return Sem(st.enter_context(nc.semaphore(name)))

        xfm = sb("xfm", [128, DC, T], BF16)
        afm = sb("afm", [128, DC, T], BF16)
        ytm = sb("ytm", [128, NT, D], BF16)
        ar = sb("ar", [128, 90112], U8)

        def carve(off, shape, dt):
            nb = int(np.prod(shape[1:])) * {F32: 4, BF16: 2, I16: 2, I32: 4}[dt]
            v = ar[:, off:off + nb].bitcast(dt)
            if len(shape) == 3:
                v = v.rearrange("p (a b) -> p a b", a=shape[1])
            return v

        qraw = carve(0, [128, T], BF16)
        kraw = carve(4096, [128, T], BF16)
        sqf = carve(8192, [128, T], F32)
        qh = [carve(16384 + i * 4096, [128, T], BF16) for i in range(2)]
        kh = [carve(24576 + i * 4096, [128, T], BF16) for i in range(2)]
        vtmp = carve(32768, [128, T], BF16)
        ptb = [carve(36864 + i * 4096, [128, T], BF16) for i in range(2)]
        rzb = carve(45056, [128, T], F32)
        oddtmp = carve(53248, [128, T], BF16)
        vtm = [carve(57344 + i * 2176, [128, NT, 68], BF16) for i in range(2)]
        wq = [carve(61696 + i * 3072, [128, DC, 192], BF16) for i in range(2)]
        rqrow = carve(68608, [128, T], F32)[0:1, :]
        rzrow = carve(76800, [128, T], F32)[0:1, :]
        wp = [carve(0 + i * 8192, [128, DC, 512], BF16) for i in range(2)]
        bpb = carve(16384, [128, D], F32)
        xrow = [carve(20480 + i * 4096, [128, D], F32) for i in range(2)]
        v1t = carve(49152, [128, D], F32)
        sqscr = carve(53248, [128, D], F32)
        outstg = [carve(36864 + i * 4096, [128, D], F32) for i in range(2)]
        hbuf = carve(0, [128, FC, 1024], BF16)
        w1s = [carve(65536 + i * 2048, [128, DC, 128], BF16) for i in range(2)]
        w2s = [carve(69632 + i * 2048, [128, D], BF16) for i in range(2)]
        yeb = [carve(73728 + i * 2048, [128, 1024], BF16) for i in range(2)]
        rb16 = carve(77824, [128, 64, 32], I16)
        rbc = carve(81920, [128, 64], I16)
        rbc2 = carve(82176, [128, 32], BF16)
        g2buf = carve(0, [128, NT, D], BF16)
        gttT = carve(82432, [128, T], BF16)
        ffns = carve(45056, [128, D], F32)
        xe = [afm.rearrange("p a b -> p (a b)")[:, i * 8192:(i + 1) * 8192]
              .rearrange("p (c n) -> p c n", c=DC) for i in range(2)]
        g1buf = afm.rearrange("p a b -> p (a b)").rearrange("p (a d) -> p a d", a=NT)

        ident_f = sb("ident_f_sb", [128, 128], F32)
        ident_b = sb("ident_b_sb", [128, 128], BF16)
        tri = sb("tri_sb", [128, 128], F32)
        ones2d = sb("ones2d_sb", [128, 128], F32)
        iota8 = sb("iota8_sb", [128, 8], F32)
        tcol = sb("tcol_sb", [128, NT], F32)
        pay1 = sb("pay1_sb", [128, NT, 16], I16)
        pay2 = sb("pay2_sb", [128, NT, 16], I16)
        ididx = sb("ididx_sb", [128, T // 16], I16)
        bqkv = sb("bqkv_sb", [64, 3 * H], F32)
        qn = sb("qn_sb", [64, 1], F32)
        kn = sb("kn_sb", [64, 1], F32)
        qnx = sb("qnx_sb", [64, 1], F32)
        ones64 = sb("ones64_sb", [64, 1], F32)
        negc = sb("negc_sb", [128, 1], F32)
        b1c = sb("b1c_sb", [128, E, FC], F32)
        wgt = sb("wg_sb", [128, DC, E], BF16)
        b2bf = sb("b2bf_sb", [8, D], BF16)
        onesrow = sb("onesrow_sb", [1, 128], F32)
        zeroi = sb("zeroi_sb", [128, 512], I16)
        cbias = sb("cbias_sb", [128, 4], F32)
        nc.const_aps.register(F32, 0.0, cbias[:, 0:1])
        nc.const_aps.register(F32, 1e-6, cbias[:, 1:2])
        nc.const_aps.register(F32, 1e-5, cbias[:, 2:3])
        logit = sb("logit_sb", [128, NT, E], F32)
        mx8 = sb("mx8_sb", [128, 8], F32)
        mi8 = sb("mi8_sb", [128, 8], U32)
        r_i1 = sb("r_i1", [128, NT], F32)
        r_i2 = sb("r_i2", [128, NT], F32)
        r_w1 = sb("r_w1", [128, NT], F32)
        r_w2 = sb("r_w2", [128, NT], F32)
        r_lr1 = sb("r_lr1", [128, NT], F32)
        r_lr2 = sb("r_lr2", [128, NT], F32)
        r_to2 = sb("r_to2", [128, NT], F32)
        m1buf = sb("m1buf", [128, NT, E], F32)
        m2buf = sb("m2buf", [128, NT, E], F32)
        msum1 = sb("msum1", [128, E], F32)
        msum2 = sb("msum2", [128, E], F32)
        tot1 = sb("tot1", [128, E], F32)
        e8a = sb("e8a", [128, 8], F32)
        e8b = sb("e8b", [128, 8], F32)
        c1a = sb("c1a", [128, 1], F32)
        c1b = sb("c1b", [128, 1], F32)
        c1c = sb("c1c", [128, 1], F32)
        c1d = sb("c1d", [128, 1], F32)
        rsc = sb("rsc", [128, NT], F32)
        rsc2 = sb("rsc2", [128, NT], F32)
        rsc3 = sb("rsc3", [128, NT], F32)
        rsc4 = sb("rsc4", [128, NT], F32)
        i32a = sb("i32a", [128, NT], I32)
        i32b = sb("i32b", [128, NT], I32)
        i32c = sb("i32c", [128, NT], I32)
        i32d = sb("i32d", [128, NT], I32)
        gmbuf = sb("gmbuf", [128, NT, E], F32)
        gk1 = sb("gk1", [128, NT], F32)
        gk2 = sb("gk2", [128, NT], F32)
        scidx = sb("scidx", [128, 32], F32)
        sc1_i = sb("sc1_i", [128, 128], I16)
        sc2_i = sb("sc2_i", [128, 128], I16)
        gidx1 = sb("gidx1", [128, 128], I16)
        gidx2 = sb("gidx2", [128, 128], I16)
        gi16 = sb("gi16", [128, NT], I16)
        gtmp = sb("gtmp", [64, 128], I16)
        xeidx = [sb(f"xeidx{i}", [128, 64], I16) for i in range(E)]
        mean_c = sb("mean_c", [128, 1], F32)
        rstd_c = sb("rstd_c", [128, 1], F32)
        ssum_c = sb("ssum_c", [128, 1], F32)
        ssq_c = sb("ssq_c", [128, 1], F32)

        pA = mkpsum("pA", [128, 2048])
        pB = mkpsum("pB", [128, 1024])
        pC = mkpsum("pC", [128, 512])
        pD = mkpsum("pD", [128, 512])
        pD_b = pD.bitcast(BF16)

        def bank(k):
            if k < 4:
                return pA[:, k * 512:(k + 1) * 512]
            if k < 6:
                return pB[:, (k - 4) * 512:(k - 3) * 512]
            return pC if k == 6 else pD

        def sbank(co):
            return [pB[:, 0:512], pB[:, 512:1024], pC, pD][co]

        sCst = mksem("sCst")
        sX = [mksem("sX0"), mksem("sX1")]
        sOdd = mksem("sOdd")
        sGi = mksem("sGi")
        sY = mksem("sY")
        sYe = [mksem("sYe0"), mksem("sYe1")]
        sOut = [mksem("sOut0"), mksem("sOut1")]
        sBp = mksem("sBp")
        sCst2 = mksem("sCst2")
        sWq = mksem("sWq")
        sYfm = mksem("sYfm")
        sXe = [mksem("sXe0"), mksem("sXe1")]
        sW1 = [mksem("sW10"), mksem("sW11")]
        sW2 = [mksem("sW20"), mksem("sW21")]
        sWp = mksem("sWp")
        sPool = mksem("sPool")
        sT = mksem("sT")
        sV = mksem("sV")
        sA = mksem("sA")

        blk = st.enter_context(nc.Block())

        def run(engine_name, f):
            out = []

            def wrap(eng):
                out.append(f(eng))
            getattr(blk, engine_name)(wrap)
            return out[0] if out else None

        M = {}

        def ph0_sync(sync):
            for dst, src in [(ident_f, ident_f_c), (ident_b, ident_b_c), (tri, tri_c),
                             (ones2d, ones2d_c), (iota8, iota8_c), (tcol, tcol_c),
                             (pay1, pay1_c), (pay2, pay2_c), (ididx, idident_c),
                             (bqkv, bqkv_d), (qn, qn_d), (kn, kn_d)]:
                sCst.inc(sync.dma_start(out=dst, in_=src), 16)
            sCst.inc(sync.dma_start(out=b1c, in_=b1_d.transpose([1, 0, 2])), 16)
            return sCst.n
        M["w_consts"] = run("sync", ph0_sync)

        RV = {}

        def ph0_gp(gp):
            gp.load_library(_mlp_lib)
            RV[128] = gp.to_reg(128)
            RV[1024] = gp.to_reg(1024)
            sCst2.inc(gp.dma_start(out=wgt, in_=wg_d.rearrange("(c p) e -> p c e", p=128)), 16)
            sCst2.inc(gp.dma_start(out=b2bf, in_=b2_d), 16)
            for sec in range(3):
                sWq.inc(gp.dma_start(
                    out=wq[0][:, :, sec * 64:(sec + 1) * 64],
                    in_=wqkvT_d.rearrange("(c p) q -> p c q", p=128)[:, :, sec * D: sec * D + 64]), 16)
            return sWq.n
        M[("g_wq", 0)] = run("gpsimd", ph0_gp)

        def ph0_vec(v):
            v.wait_ge(sCst.h, M["w_consts"])
            v.memset(onesrow, 1.0)
            v.memset(ones64, 1.0)
            v.memset(cbias[:, 0:1], 0.0)
            v.memset(cbias[:, 1:2], 1e-6)
            v.memset(cbias[:, 2:3], 1e-5)
            v.tensor_scalar(qnx, qn, 1.0 / 8.0, None, OP.mult)
            for i in range(2):
                v.memset(vtm[i][:, :, 65:68], 0.0)
                v.memset(vtm[i][:, :, 64:65], 1.0)
            v.memset(zeroi, 0)
            v.memset(sc1_i, 0)
            v.memset(sc2_i, 0)
            v.memset(gidx1, 0)
            v.memset(gidx2, 0)
            for i in range(E):
                v.memset(xeidx[i], 0)
            v.memset(msum1, 0.0)
            return sV.inc(v.memset(msum2, 0.0))
        M["v_init"] = run("vector", ph0_vec)

        def ph0_ten(te):
            te.wait_ge(sCst.h, M["w_consts"])
            te.transpose(pD[0:1, 0:64], qn, ident_f[0:64, 0:64])
            return sT.inc(te.transpose(pD[0:1, 64:128], kn, ident_f[0:64, 0:64]))
        M["t_qnk"] = run("tensor", ph0_ten)

        def ph0_vec2(v):
            v.wait_ge(sT.h, M["t_qnk"])
            v.reduce_max(c1a[0:1, :], pD[0:1, 0:64], axis=AX.X, apply_absolute_value=True)
            v.reduce_max(c1b[0:1, :], pD[0:1, 64:128], axis=AX.X, apply_absolute_value=True)
            v.drain()
            return sV.inc(v.scalar_tensor_tensor(c1c[0:1, :], c1a[0:1, :], -8.0,
                                                 c1b[0:1, :], OP.mult, OP.mult))
        M["v_negc"] = run("vector", ph0_vec2)

        def ph0_ten2(te):
            te.wait_ge(sV.h, max(M["v_negc"], M["v_init"]))
            return sT.inc(te.matmul(pC[:, 0:1], onesrow, c1c[0:1, :], start=True, stop=True))
        M["t_negc"] = run("tensor", ph0_ten2)

        def ph0_vec3(v):
            v.wait_ge(sT.h, M["t_negc"])
            return sV.inc(v.tensor_copy(negc, pC[:, 0:1]))
        M["v_negc2"] = run("vector", ph0_vec3)

        for f in range(NT):
            par = f % 2

            def ldx(sync, f=f, par=par):
                if f >= 2:
                    sync.wait_ge(sT.h, M[("t_x", f - 2)])
                return sX[par].inc(sync.dma_start(out=xrow[par], in_=x_d[f * 128:(f + 1) * 128, :]), 16)
            M[("w_x", f)] = run("sync", ldx)

            def trx(te, f=f, par=par):
                te.wait_ge(sX[par].h, M[("w_x", f)])
                if f == 0:
                    te.wait_ge(sV.h, M["v_negc2"])
                if f >= 2:
                    te.wait_ge(sV.h, M[("v_x", f - 2)])
                half = (f % 2) * 1024
                last = None
                for dc in range(DC):
                    last = te.transpose(pA[:, half + dc * 128: half + (dc + 1) * 128],
                                        xrow[par][:, dc * 128:(dc + 1) * 128], ident_f)
                return sT.inc(last)
            M[("t_x", f)] = run("tensor", trx)

            def cpx(v, f=f):
                v.wait_ge(sT.h, M[("t_x", f)])
                half = (f % 2) * 1024
                return sV.inc(v.tensor_copy(
                    xfm[:, :, f * 128:(f + 1) * 128],
                    pA[:, half:half + 1024].rearrange("p (a b) -> p a b", a=DC)))
            M[("v_x", f)] = run("vector", cpx)

        for h in range(H):
            par = h % 2

            if h + 1 < H:
                def pre(gp, h=h):
                    gp.wait_ge(sWq.h, M[("g_wq", h)])
                    if h >= 1:
                        gp.wait_ge(sT.h, M[("t_vmm", h - 1)])
                    for sec in range(3):
                        col = sec * D + (h + 1) * 64
                        sWq.inc(gp.dma_start(
                            out=wq[(h + 1) % 2][:, :, sec * 64:(sec + 1) * 64],
                            in_=wqkvT_d.rearrange("(c p) q -> p c q", p=128)[:, :, col:col + 64]), 16)
                    return sWq.n
                M[("g_wq", h + 1)] = run("gpsimd", pre)

            for sec in range(2):
                raw = qraw if sec == 0 else kraw
                hat = qh[par] if sec == 0 else kh[par]
                ncol = qnx if sec == 0 else kn

                def mmqk(te, h=h, par=par, sec=sec):
                    te.wait_ge(sWq.h, M[("g_wq", h)])
                    if h == 0 and sec == 0:
                        te.wait_ge(sV.h, max(M[("v_x", NT - 1)], M["v_negc2"]))
                    elif sec == 0:
                        te.wait_ge(sV.h, M[("v_afm", h - 1)])
                    else:
                        te.wait_ge(sV.h, M[("v_sq", h, 0)])
                    last = None
                    for co in range(4):
                        for dc in range(DC):
                            last = te.matmul(pA[0:64, co * 512:(co + 1) * 512],
                                             wq[par][:, dc, sec * 64:sec * 64 + 64],
                                             xfm[:, dc, co * 512:(co + 1) * 512],
                                             start=(dc == 0), stop=(dc == DC - 1))
                    return sT.inc(last)
                M[("t_qk", h, sec)] = run("tensor", mmqk)

                def evqk(v, h=h, sec=sec, raw=raw):
                    v.wait_ge(sT.h, M[("t_qk", h, sec)])
                    if sec == 1:
                        v.wait_ge(sT.h, M[("t_ssq", h, 0)])
                    bc = bqkv[:, sec * H + h: sec * H + h + 1]
                    v.tensor_scalar(raw[0:64, 0:1024], pA[0:64, 0:1024], bc, None, OP.add)
                    v.tensor_scalar(raw[0:64, 1024:2048], pA[0:64, 1024:2048], bc, None, OP.add)
                    v.drain()
                    v.tensor_tensor(sqf[0:64, 0:1024], raw[0:64, 0:1024], raw[0:64, 0:1024], OP.mult)
                    return sV.inc(v.tensor_tensor(sqf[0:64, 1024:2048], raw[0:64, 1024:2048],
                                                  raw[0:64, 1024:2048], OP.mult))
                M[("v_sq", h, sec)] = run("vector", evqk)

                def mmsq(te, h=h, sec=sec):
                    te.wait_ge(sV.h, M[("v_sq", h, sec)])
                    if sec == 1:
                        te.wait_ge(sV.h, M[("v_hat", h, 0)])
                    last = None
                    for co in range(4):
                        last = te.matmul(sbank(co)[0:1, :], ones64,
                                         sqf[0:64, co * 512:(co + 1) * 512], start=True, stop=True)
                    return sT.inc(last)
                M[("t_ssq", h, sec)] = run("tensor", mmsq)

                def actrq(sc, h=h, sec=sec):
                    sc.wait_ge(sT.h, M[("t_ssq", h, sec)])
                    for co in range(4):
                        sc.activation(rqrow[0:1, co * 512:(co + 1) * 512], sbank(co)[0:1, :],
                                      AF.Ln, bias=1e-6, scale=1.0 / 64.0)
                    sc.drain()
                    last = None
                    for co in range(4):
                        last = sc.activation(rqrow[0:1, co * 512:(co + 1) * 512],
                                             rqrow[0:1, co * 512:(co + 1) * 512],
                                             AF.Exp, bias=0.0, scale=-0.5)
                    return sA.inc(last)
                M[("a_rq", h, sec)] = run("scalar", actrq)

                def mmbc(te, h=h, sec=sec):
                    te.wait_ge(sA.h, M[("a_rq", h, sec)])
                    last = None
                    for co in range(4):
                        last = te.matmul(sbank(co)[0:64, :], onesrow[0:1, 0:64],
                                         rqrow[0:1, co * 512:(co + 1) * 512], start=True, stop=True)
                    return sT.inc(last)
                M[("t_bc", h, sec)] = run("tensor", mmbc)

                def evhat(v, h=h, sec=sec, raw=raw, hat=hat, ncol=ncol):
                    v.wait_ge(sT.h, M[("t_bc", h, sec)])
                    last = None
                    for co in range(4):
                        last = v.scalar_tensor_tensor(hat[0:64, co * 512:(co + 1) * 512],
                                                      raw[0:64, co * 512:(co + 1) * 512], ncol,
                                                      sbank(co)[0:64, :], OP.mult, OP.mult)
                    return sV.inc(last)
                M[("v_hat", h, sec)] = run("vector", evhat)

            def mmv(te, h=h, par=par):
                te.wait_ge(sV.h, M[("v_sq", h, 1)])
                last = None
                for co in range(4):
                    for dc in range(DC):
                        last = te.matmul(pA[0:64, co * 512:(co + 1) * 512],
                                         wq[par][:, dc, 128:192],
                                         xfm[:, dc, co * 512:(co + 1) * 512],
                                         start=(dc == 0), stop=(dc == DC - 1))
                return sT.inc(last)
            M[("t_vmm", h)] = run("tensor", mmv)

            def evv(v, h=h):
                v.wait_ge(sT.h, M[("t_vmm", h)])
                return sV.inc(v.tensor_scalar(vtmp[0:64, :], pA[0:64, :],
                                              bqkv[:, 2 * H + h: 2 * H + h + 1], None, OP.add))
            M[("v_vtmp", h)] = run("vector", evv)

            def trv(te, h=h):
                te.wait_ge(sV.h, M[("v_vtmp", h)])
                if h > 0:
                    te.wait_ge(sV.h, M[("v_vtm", h - 1)])
                last = None
                for stt in range(NT):
                    last = te.transpose(pD_b[:, stt * 64:(stt + 1) * 64],
                                        vtmp[0:64, stt * 128:(stt + 1) * 128],
                                        ident_b[0:64, 0:64])
                return sT.inc(last)
            M[("t_vt", h)] = run("tensor", trv)

            def evvt(v, h=h, par=par):
                v.wait_ge(sT.h, M[("t_vt", h)])
                return sV.inc(v.tensor_copy(vtm[par][:, :, 0:64],
                                            pD_b.rearrange("p (a b) -> p a b", a=NT)))
            M[("v_vtm", h)] = run("vector", evvt)

            for stp in range(NT):
                def mmst(te, h=h, par=par, stp=stp):
                    if stp == 0:
                        te.wait_ge(sV.h, M[("v_hat", h, 1)])
                        te.wait_ge(sV.h, M[("v_vtm", h)])
                    if stp >= 1:
                        te.wait_ge(sA.h, M[("a_exp", h, stp - 1)])
                    last = None
                    for co in range(4):
                        last = te.matmul(sbank(co)[:, :],
                                         kh[par][0:64, stp * 128:(stp + 1) * 128],
                                         qh[par][0:64, co * 512:(co + 1) * 512],
                                         start=True, stop=True)
                    sT.inc(last)
                    if stp >= 1:
                        lastpv = None
                        for co in range(4):
                            lastpv = te.matmul(pA[0:65, co * 512:(co + 1) * 512],
                                               vtm[par][:, stp - 1, 0:65],
                                               ptb[(stp - 1) % 2][:, co * 512:(co + 1) * 512],
                                               start=(stp - 1 == 0), stop=False)
                        M[("t_pv", h, stp - 1)] = sT.inc(lastpv)
                    return sT.n
                M[("t_st", h, stp)] = run("tensor", mmst)

                def aexp(sc, h=h, stp=stp):
                    sc.wait_ge(sT.h, M[("t_st", h, stp)])
                    if stp >= 2:
                        sc.wait_ge(sT.h, M[("t_pv", h, stp - 2)])
                    last = None
                    for co in range(4):
                        last = sc.activation(ptb[stp % 2][:, co * 512:(co + 1) * 512],
                                             sbank(co)[:, :], AF.Exp, bias=negc, scale=1.0)
                    return sA.inc(last)
                M[("a_exp", h, stp)] = run("scalar", aexp)

            def mmpvl(te, h=h, par=par):
                te.wait_ge(sA.h, M[("a_exp", h, NT - 1)])
                last = None
                for co in range(4):
                    last = te.matmul(pA[0:65, co * 512:(co + 1) * 512],
                                     vtm[par][:, NT - 1, 0:65],
                                     ptb[(NT - 1) % 2][:, co * 512:(co + 1) * 512],
                                     start=False, stop=True)
                return sT.inc(last)
            M[("t_pv", h, NT - 1)] = run("tensor", mmpvl)

            def vrz(v, h=h):
                v.wait_ge(sT.h, M[("t_pv", h, NT - 1)])
                return sV.inc(v.reciprocal(rzrow, pA[64:65, :]))
            M[("v_rz", h)] = run("vector", vrz)

            def mmrz(te, h=h):
                te.wait_ge(sV.h, M[("v_rz", h)])
                last = None
                for co in range(4):
                    last = te.matmul(sbank(co)[0:64, :], onesrow[0:1, 0:64],
                                     rzrow[0:1, co * 512:(co + 1) * 512], start=True, stop=True)
                return sT.inc(last)
            M[("t_rz", h)] = run("tensor", mmrz)

            def vnorm(v, h=h):
                v.wait_ge(sT.h, M[("t_rz", h)])
                if h % 2 == 1 and h >= 3:
                    v.wait_ge(sOdd.h, M[("w_odd", h - 2)])
                for co in range(4):
                    v.tensor_copy(rzb[0:64, co * 512:(co + 1) * 512], sbank(co)[0:64, :])
                v.drain()
                dst = afm[0:64, h // 2, :] if h % 2 == 0 else oddtmp[0:64, :]
                return sV.inc(v.scalar_tensor_tensor(dst, pA[0:64, :], 1.0, rzb[0:64, :],
                                                     OP.mult, OP.mult))
            M[("v_afm", h)] = run("vector", vnorm)

            if h % 2 == 1:
                def dmodd(sync, h=h):
                    sync.wait_ge(sV.h, M[("v_afm", h)])
                    return sOdd.inc(sync.dma_start(out=afm[64:128, h // 2, :], in_=oddtmp[0:64, :]), 16)
                M[("w_odd", h)] = run("sync", dmodd)

        def gpre(gp):
            gp.wait_ge(sV.h, M[("v_hat", H - 1, 1)])
            for oc in range(2):
                sWp.inc(gp.dma_start(
                    out=wp[oc],
                    in_=wpT_d.rearrange("(c p) o -> p c o", p=128)[:, :, oc * 512:(oc + 1) * 512]), 16)
            for k in range(25):
                rows = 512 if k < 24 else 128
                sPool.inc(gp.dma_start(
                    out=tok_pool[k * 512:k * 512 + rows, :].rearrange("(p a) e -> p (a e)", p=128),
                    in_=zeroi[:, 0:rows]), 16)
            for half in range(2):
                sPool.inc(gp.dma_start(out=ye_pool[E * CAP:E * CAP + 128, half * 512:(half + 1) * 512],
                                       in_=zeroi[:, :].bitcast(BF16)), 16)
            return sWp.n
        M["g_proj"] = run("gpsimd", gpre)
        M["g_pool0"] = sPool.n

        def wbp(sync):
            sync.wait_ge(sV.h, M[("v_afm", H - 1)])
            return sBp.inc(sync.dma_start(out=rzrow[0:1, 0:D], in_=bp_d), 16)
        M["w_bp"] = run("sync", wbp)

        def tbp(te):
            te.wait_ge(sBp.h, M["w_bp"])
            te.wait_ge(sV.h, M[("v_afm", H - 1)])
            last = None
            for oc in range(2):
                last = te.matmul(sbank(oc)[:, :], onesrow, rzrow[0:1, oc * 512:(oc + 1) * 512],
                                 start=True, stop=True)
            return sT.inc(last)
        M["t_bp"] = run("tensor", tbp)

        def vbp(v):
            v.wait_ge(sT.h, M["t_bp"])
            v.tensor_copy(bpb[:, 0:512], sbank(0)[:, :])
            return sV.inc(v.tensor_copy(bpb[:, 512:1024], sbank(1)[:, :]))
        M["v_bp"] = run("vector", vbp)

        for lt in range(NT):
            par = lt % 2

            def ldx2(sync, lt=lt, par=par):
                if lt >= 2:
                    sync.wait_ge(sV.h, M[("v_ln1", lt - 2)])
                else:
                    sync.wait_ge(sT.h, M[("t_pv", H - 1, NT - 1)])
                return sX[par].inc(sync.dma_start(out=xrow[par], in_=x_d[lt * 128:(lt + 1) * 128, :]), 16)
            M[("w_x2", lt)] = run("sync", ldx2)

            def mmproj(te, lt=lt):
                if lt == 0:
                    te.wait_ge(sWp.h, M["g_proj"])
                    te.wait_ge(sOdd.h, M[("w_odd", H - 1)])
                if lt >= 2:
                    te.wait_ge(sV.h, M[("v_ln1", lt - 2)])
                half = (lt % 2) * 1024
                last = None
                for oc in range(2):
                    for dc in range(DC):
                        last = te.matmul(pA[:, half + oc * 512: half + (oc + 1) * 512],
                                         afm[:, dc, lt * 128:(lt + 1) * 128],
                                         wp[oc][:, dc, :],
                                         start=(dc == 0), stop=(dc == DC - 1))
                return sT.inc(last)
            M[("t_proj", lt)] = run("tensor", mmproj)

            def vln1(v, lt=lt, par=par):
                v.wait_ge(sT.h, M[("t_proj", lt)])
                v.wait_ge(sX[par].h, M[("w_x2", lt)])
                if lt == 0:
                    v.wait_ge(sV.h, M["v_bp"])
                half = (lt % 2) * 1024
                for u in range(2):
                    v.tensor_tensor(v1t[:, u * 512:(u + 1) * 512],
                                    pA[:, half + u * 512:half + (u + 1) * 512],
                                    bpb[:, u * 512:(u + 1) * 512], OP.add)
                v.drain()
                for u in range(2):
                    v.tensor_tensor(v1t[:, u * 512:(u + 1) * 512], v1t[:, u * 512:(u + 1) * 512],
                                    xrow[par][:, u * 512:(u + 1) * 512], OP.add)
                v.drain()
                for u in range(2):
                    v.scalar_tensor_tensor(sqscr[:, u * 512:(u + 1) * 512],
                                           v1t[:, u * 512:(u + 1) * 512], 1.0,
                                           v1t[:, u * 512:(u + 1) * 512], OP.mult, OP.mult)
                v.reduce_sum(ssum_c, v1t, axis=AX.X)
                v.drain()
                v.reduce_sum(ssq_c, sqscr, axis=AX.X)
                v.drain()
                v.tensor_scalar(mean_c, ssum_c, 1.0 / D, None, OP.mult)
                v.tensor_scalar(c1d, ssq_c, 1.0 / D, None, OP.mult)
                v.drain()
                return sV.inc(v.tensor_scalar(c1c, mean_c, mean_c, -1.0, OP.mult, OP.mult))
            M[("v_var", lt)] = run("vector", vln1)

            def aln1(sc, lt=lt):
                sc.wait_ge(sV.h, M[("v_var", lt)])
                sc.activation(c1d, c1d, AF.Ln, bias=c1c, scale=1.0) if False else None
                sc.activation(rstd_c, c1d, AF.Ln, bias=c1c, scale=1.0)
                sc.drain()
                return sA.inc(sc.activation(rstd_c, rstd_c, AF.Exp, bias=0.0, scale=-0.5))
            M[("a_ln1", lt)] = run("scalar", aln1)

            def vy(v, lt=lt):
                v.wait_ge(sA.h, M[("a_ln1", lt)])
                return sV.inc(v.tensor_scalar(ytm[:, lt, :], v1t, mean_c, rstd_c,
                                              OP.subtract, OP.mult))
            M[("v_ln1", lt)] = run("vector", vy)

            def wy(sync, lt=lt):
                sync.wait_ge(sV.h, M[("v_ln1", lt)])
                return sY.inc(sync.dma_start(out=y_pool[lt * 128:(lt + 1) * 128, :],
                                             in_=ytm[:, lt, :]), 16)
            M[("w_y", lt)] = run("sync", wy)

        def gyfm(gp):
            gp.wait_ge(sY.h, M[("w_y", NT - 1)])
            gp.wait_ge(sCst.h, M["w_consts"])
            yfm4 = xfm.rearrange("p a b -> p (a b)").rearrange("p (c d j) -> p c d j", c=16, d=DC)
            for c in range(16):
                sYfm.inc(gp.dma_gather(yfm4[:, c, :, :], y_pool,
                                       ididx[:, c * 8:(c + 1) * 8], 128, RV[128], D,
                                       transpose=True), 16)
                gp.wait_ge(sYfm.h, sYfm.n)
            return sYfm.n
        M["g_yfm"] = run("gpsimd", gyfm)

        for lt in range(NT):
            def mmlg(te, lt=lt):
                if lt == 0:
                    te.wait_ge(sYfm.h, M["g_yfm"])
                    te.wait_ge(sCst2.h, M["w_consts2"] if "w_consts2" in M else sCst2.n)
                    te.wait_ge(sV.h, M[("v_ln1", NT - 1)])
                if lt >= 4:
                    te.wait_ge(sV.h, M[("v_lg", lt - 4)])
                yfm4 = xfm.rearrange("p a b -> p (a b)").rearrange("p (c d j) -> p c d j", c=16, d=DC)
                last = None
                for dc in range(DC):
                    last = te.matmul(sbank(lt % 4)[:, 0:8],
                                     yfm4[:, lt, dc, :],
                                     wgt[:, dc, :], start=(dc == 0), stop=(dc == DC - 1))
                return sT.inc(last)
            M[("t_lg", lt)] = run("tensor", mmlg)

            def vlg(v, lt=lt):
                v.wait_ge(sT.h, M[("t_lg", lt)])
                return sV.inc(v.tensor_copy(logit[:, lt, :], sbank(lt % 4)[:, 0:8]))
            M[("v_lg", lt)] = run("vector", vlg)

        for lt in range(NT):
            def vrt(v, lt=lt):
                L = logit[:, lt, :]
                v.drain()
                v.max(mx8, L)
                v.drain()
                v.max_index(mi8, mx8, L)
                v.drain()
                v.tensor_copy(r_i1[:, lt:lt + 1], mi8[:, 0:1])
                v.tensor_copy(r_i2[:, lt:lt + 1], mi8[:, 1:2])
                v.tensor_scalar(c1a, mx8[:, 0:1], -1.0, None, OP.mult)
                v.tensor_tensor(c1b, mx8[:, 1:2], mx8[:, 0:1], OP.subtract)
                v.drain()
                v.tensor_scalar(m1buf[:, lt, :], iota8, r_i1[:, lt:lt + 1], None, OP.is_equal)
                return sV.inc(v.tensor_scalar(m2buf[:, lt, :], iota8, r_i2[:, lt:lt + 1], None, OP.is_equal))
            M[("v_rt1", lt)] = run("vector", vrt)

            def art(sc, lt=lt):
                sc.wait_ge(sV.h, M[("v_rt1", lt)])
                sc.activation(e8a, logit[:, lt, :], AF.Exp, bias=c1a, scale=1.0, accum_out=ssum_c)
                return sA.inc(sc.activation(c1c, c1b, AF.Exp, bias=0.0, scale=1.0))
            M[("a_rt", lt)] = run("scalar", art)

            def vrt2(v, lt=lt):
                v.wait_ge(sA.h, M[("a_rt", lt)])
                v.reciprocal(c1a, ssum_c)
                v.drain()
                v.tensor_tensor(c1b, c1c, c1a, OP.mult)
                v.drain()
                v.scalar_tensor_tensor(c1d, c1a, 1e-9, c1b, OP.add, OP.add)
                v.drain()
                v.reciprocal(c1d, c1d)
                v.drain()
                v.tensor_tensor(r_w1[:, lt:lt + 1], c1a, c1d, OP.mult)
                return sV.inc(v.tensor_tensor(r_w2[:, lt:lt + 1], c1b, c1d, OP.mult))
            M[("v_rt2", lt)] = run("vector", vrt2)

            def mmrk(te, lt=lt):
                te.wait_ge(sV.h, M[("v_rt2", lt)])
                te.matmul(pC[:, 0:8], tri, m1buf[:, lt, :], start=True, stop=False)
                te.matmul(pC[:, 0:8], ones2d, msum1, start=False, stop=True)
                te.matmul(pD[:, 0:8], tri, m2buf[:, lt, :], start=True, stop=False)
                return sT.inc(te.matmul(pD[:, 0:8], ones2d, msum2, start=False, stop=True))
            M[("t_rk", lt)] = run("tensor", mmrk)

            def vrk(v, lt=lt):
                v.wait_ge(sT.h, M[("t_rk", lt)])
                v.tensor_tensor(e8a, pC[:, 0:8], m1buf[:, lt, :], OP.mult)
                v.tensor_tensor(e8b, pD[:, 0:8], m2buf[:, lt, :], OP.mult)
                v.drain()
                v.reduce_sum(r_lr1[:, lt:lt + 1], e8a, axis=AX.X)
                v.reduce_sum(r_lr2[:, lt:lt + 1], e8b, axis=AX.X)
                v.tensor_tensor(msum1, msum1, m1buf[:, lt, :], OP.add)
                return sV.inc(v.tensor_tensor(msum2, msum2, m2buf[:, lt, :], OP.add))
            M[("v_rk", lt)] = run("vector", vrk)

        def mmtot(te):
            te.wait_ge(sV.h, M[("v_rk", NT - 1)])
            return sT.inc(te.matmul(pC[:, 0:8], ones2d, msum1, start=True, stop=True))
        M["t_tot"] = run("tensor", mmtot)

        def vtot(v):
            v.wait_ge(sT.h, M["t_tot"])
            v.tensor_copy(tot1, pC[:, 0:8])
            v.drain()
            for lt in range(NT):
                v.tensor_tensor(gmbuf[:, lt, :], tot1, m2buf[:, lt, :], OP.mult)
            v.drain()
            for lt in range(NT):
                v.reduce_sum(r_to2[:, lt:lt + 1], gmbuf[:, lt, :], axis=AX.X)
            v.drain()
            v.tensor_tensor(r_lr2, r_lr2, r_to2, OP.add)
            v.tensor_scalar(gk1, r_lr1, float(CAP), None, OP.is_lt)
            v.drain()
            v.tensor_scalar(gk2, r_lr2, float(CAP), None, OP.is_lt)
            v.tensor_tensor(r_w1, r_w1, gk1, OP.mult)
            v.tensor_scalar(rsc, r_lr1, float(CAP - 1), None, OP.min)
            v.drain()
            v.tensor_tensor(r_w2, r_w2, gk2, OP.mult)
            v.tensor_scalar(rsc2, r_lr2, float(CAP - 1), None, OP.min)
            v.drain()
            v.scalar_tensor_tensor(rsc, r_i1, float(CAP), rsc, OP.mult, OP.add)
            v.scalar_tensor_tensor(rsc2, r_i2, float(CAP), rsc2, OP.mult, OP.add)
            v.drain()
            v.tensor_scalar(rsc3, rsc, -float(E * CAP), None, OP.add)
            v.tensor_scalar(rsc4, rsc2, -float(E * CAP), None, OP.add)
            v.drain()
            v.tensor_tensor(rsc3, rsc3, gk1, OP.mult)
            v.tensor_tensor(rsc4, rsc4, gk2, OP.mult)
            v.drain()
            v.tensor_scalar(rsc3, rsc3, float(E * CAP), None, OP.add)
            v.tensor_scalar(rsc4, rsc4, float(E * CAP), None, OP.add)
            v.drain()
            return sV.inc(v.tensor_copy(gi16, rsc3))
        M["v_gi1"] = M["v_gi2pre"] = run("vector", vtot)

        def wgi1(sync):
            sync.wait_ge(sV.h, M["v_gi1"])
            return sGi.inc(sync.dma_start(out=tokinfo[0:1, :].rearrange("a (p f) -> (a p) f", p=128),
                                          in_=gi16), 16)
        M["w_gi1"] = run("sync", wgi1)

        def vgi2(v):
            v.wait_ge(sGi.h, M["w_gi1"])
            return sV.inc(v.tensor_copy(gi16, rsc4))
        M["v_gi2"] = run("vector", vgi2)

        def wgi2(sync):
            sync.wait_ge(sV.h, M["v_gi2"])
            return sGi.inc(sync.dma_start(out=tokinfo[1:2, :].rearrange("a (p f) -> (a p) f", p=128),
                                          in_=gi16), 16)
        M["w_gi2"] = run("sync", wgi2)

        def vsc(v):
            v.wait_ge(sV.h, M["v_gi2"])
            v.tensor_copy(i32a, rsc)
            v.tensor_copy(i32b, rsc2)
            v.drain()
            v.tensor_scalar(i32c, i32a, 15, None, OP.bitwise_and)
            v.tensor_scalar(i32d, i32b, 15, None, OP.bitwise_and)
            v.drain()
            v.tensor_scalar(i32a, i32a, 4, None, OP.arith_shift_right)
            v.tensor_scalar(i32b, i32b, 4, None, OP.arith_shift_right)
            v.tensor_copy(rsc3, i32c)
            v.tensor_copy(rsc4, i32d)
            v.drain()
            v.tensor_copy(rsc, i32a)
            v.tensor_copy(rsc2, i32b)
            v.drain()
            v.scalar_tensor_tensor(rsc3, rsc3, 512.0, rsc, OP.mult, OP.add)
            v.scalar_tensor_tensor(rsc4, rsc4, 512.0, rsc2, OP.mult, OP.add)
            v.drain()
            v.tensor_scalar(rsc3, rsc3, 1.0, None, OP.add)
            v.tensor_scalar(rsc4, rsc4, 1.0, None, OP.add)
            v.tensor_scalar(rsc, tcol, float(1 + 8192), None, OP.add)
            v.tensor_scalar(rsc2, tcol, float(1 + 8192 + T), None, OP.add)
            v.drain()
            v.tensor_tensor(rsc3, rsc3, rsc, OP.subtract)
            v.tensor_tensor(rsc4, rsc4, rsc2, OP.subtract)
            v.drain()
            v.tensor_tensor(rsc3, rsc3, gk1, OP.mult)
            v.tensor_tensor(rsc4, rsc4, gk2, OP.mult)
            v.drain()
            v.tensor_tensor(rsc3, rsc3, rsc, OP.add)
            v.tensor_tensor(rsc4, rsc4, rsc2, OP.add)
            v.drain()
            v.tensor_copy(scidx[:, 0:16], rsc3)
            return sV.inc(v.tensor_copy(scidx[:, 16:32], rsc3))
        M[("v_sc", 0)] = run("vector", vsc)

        def tsc0(te):
            te.wait_ge(sV.h, M[("v_sc", 0)])
            return sT.inc(te.transpose(pD[0:32, 0:128], scidx, ident_f))
        M[("t_sc", 0)] = run("tensor", tsc0)

        def vsci0(v):
            v.wait_ge(sT.h, M[("t_sc", 0)])
            v.tensor_copy(sc1_i[0:32, :], pD[0:32, 0:128])
            v.tensor_copy(scidx[:, 0:16], rsc4)
            return sV.inc(v.tensor_copy(scidx[:, 16:32], rsc4))
        M[("v_sc", 1)] = run("vector", vsci0)

        def tsc1(te):
            te.wait_ge(sV.h, M[("v_sc", 1)])
            return sT.inc(te.transpose(pD[0:32, 0:128], scidx, ident_f))
        M[("t_sc", 1)] = run("tensor", tsc1)

        def vsci1(v):
            v.wait_ge(sT.h, M[("t_sc", 1)])
            return sV.inc(v.tensor_copy(sc2_i[0:32, :], pD[0:32, 0:128]))
        M[("v_sci", 1)] = run("vector", vsci1)

        def gsc(gp):
            gp.wait_ge(sV.h, M[("v_sci", 1)])
            gp.wait_ge(sPool.h, M["g_pool0"])
            gp.wait_ge(sCst.h, M["w_consts"])
            for half in range(2):
                sPool.inc(gp.dma_scatter_add(tok_pool[:, 0:16], pay1[:, half * 8:(half + 1) * 8, :],
                                             sc1_i[:, half * 64:(half + 1) * 64],
                                             T // 2, RV[1024], 16, elem_step=128), 16)
                gp.wait_ge(sPool.h, sPool.n)
            for half in range(2):
                sPool.inc(gp.dma_scatter_add(tok_pool[:, 0:16], pay2[:, half * 8:(half + 1) * 8, :],
                                             sc2_i[:, half * 64:(half + 1) * 64],
                                             T // 2, RV[1024], 16, elem_step=128), 16)
                gp.wait_ge(sPool.h, sPool.n)
            return sPool.n
        M["g_sc"] = run("gpsimd", gsc)

        def wgidx(sync):
            sync.wait_ge(sGi.h, M["w_gi2"])
            for arr in range(2):
                dst = gtmp[arr * 32:arr * 32 + 16, :]
                srca = tokinfo[arr:arr + 1, :].rearrange("a (u q c) -> (a q) u c", u=8, q=16)
                sGi.inc(sync.dma_start(out=dst.rearrange("q (u c) -> q u c", u=8), in_=srca), 16)
            return sGi.n
        M["w_gidxd"] = run("sync", wgidx)

        def vgidx(v):
            v.wait_ge(sGi.h, M["w_gidxd"])
            last = None
            for arr in range(2):
                dstt = gidx1 if arr == 0 else gidx2
                ina = gtmp[arr * 32:arr * 32 + 16, :].rearrange("q (u c) -> q u c", u=8).transpose([0, 2, 1])
                outa = dstt[0:16, :].rearrange("q (c u) -> q c u", c=16)
                last = v.tensor_copy(outa, ina)
            return sV.inc(last)
        M["v_gidx"] = run("vector", vgidx)

        def wgidx2(sync):
            sync.wait_ge(sV.h, M["v_gidx"])
            sGi.inc(sync.dma_start(out=gidx1[16:32, :], in_=gidx1[0:16, :]), 16)
            sGi.inc(sync.dma_start(out=gidx2[16:32, :], in_=gidx2[0:16, :]), 16)
            return sGi.n
        M["w_gidx"] = run("sync", wgidx2)

        def vgt(v):
            v.wait_ge(sV.h, M[("v_sci", 1)])
            for lt in range(NT):
                v.tensor_scalar(gmbuf[:, lt, :], m1buf[:, lt, :], r_w1[:, lt:lt + 1], None, OP.mult)
            v.drain()
            for lt in range(NT):
                sV.inc(v.scalar_tensor_tensor(m1buf[:, lt, :], m2buf[:, lt, :],
                                              r_w2[:, lt:lt + 1], gmbuf[:, lt, :], OP.mult, OP.add))
                M[("v_gm", lt)] = sV.n
            return sV.n
        run("vector", vgt)

        for lt in range(NT):
            def tgt(te, lt=lt):
                te.wait_ge(sV.h, M[("v_gm", lt)])
                if lt >= 2:
                    te.wait_ge(sV.h, M[("v_gtc", lt - 2)])
                return sT.inc(te.transpose(sbank(lt % 2)[0:8, 0:128], m1buf[:, lt, :], ident_f))
            M[("t_gt", lt)] = run("tensor", tgt)

            def vgt2(v, lt=lt):
                v.wait_ge(sT.h, M[("t_gt", lt)])
                sV.inc(v.tensor_copy(gttT[0:8, lt * 128:(lt + 1) * 128], sbank(lt % 2)[0:8, 0:128]))
                M[("v_gtc", lt)] = sV.n
                return sV.n
            run("vector", vgt2)

        for e in range(E):
            def grb(gp, e=e):
                gp.wait_ge(sPool.h, M["g_sc"])
                if e >= 1:
                    gp.wait_ge(sV.h, M[("v_rbc", e - 1)])
                src = tok_pool[1 + e * 64: 1 + e * 64 + 8192, :].rearrange("(q s) c -> q s c", q=16)
                sPool.inc(gp.dma_start(out=rb16[0:16, :, :], in_=src[:, 0:64, 0:32]), 16)
                return sPool.n
            M[("g_rb", e)] = run("gpsimd", grb)

            def vcompact(v, e=e):
                v.wait_ge(sPool.h, M[("g_rb", e)])
                if e >= 1:
                    v.wait_ge(sT.h, M[("t_rb1", e - 1)])
                return sV.inc(v.tensor_copy(rbc[0:16, :], rb16[0:16, :, 0:1].rearrange("p a b -> p (a b)")))
            M[("v_rbc", e)] = run("vector", vcompact)

            def trb(te, e=e):
                te.wait_ge(sV.h, M[("v_rbc", e)])
                if e == 0:
                    te.wait_ge(sV.h, M[("v_sci", 1)])
                else:
                    te.wait_ge(sV.h, M[("v_xi", e - 1)])
                return sT.inc(te.transpose(pD_b[0:64, 0:16], rbc[0:16, :].bitcast(BF16),
                                           ident_b[0:16, 0:16]))
            M[("t_rb1", e)] = run("tensor", trb)

            def vrb2(v, e=e):
                v.wait_ge(sT.h, M[("t_rb1", e)])
                v.tensor_copy(rbc2[0:64, 0:16], pD_b[0:64, 0:16])
                return sV.inc(v.tensor_copy(rbc2[0:64, 16:32], pD_b[0:64, 0:16]))
            M[("v_rb2", e)] = run("vector", vrb2)

            def trb2(te, e=e):
                te.wait_ge(sV.h, M[("v_rb2", e)])
                return sT.inc(te.transpose(pD_b[0:32, 0:64], rbc2[0:64, 0:32], ident_b[0:64, 0:64]))
            M[("t_rb2", e)] = run("tensor", trb2)

            def vrb3(v, e=e):
                v.wait_ge(sT.h, M[("t_rb2", e)])
                return sV.inc(v.tensor_copy(xeidx[e][0:32, :].bitcast(BF16), pD_b[0:32, 0:64]))
            M[("v_xi", e)] = run("vector", vrb3)

        for e in range(E):
            par = e % 2

            def gxe(gp, e=e, par=par):
                gp.wait_ge(sV.h, M[("v_xi", e)])
                if e <= 1:
                    gp.wait_ge(sT.h, M[("t_proj", NT - 1)])
                if e >= 2:
                    gp.wait_ge(sT.h, M[("t_w1", e - 2, FC - 1)])
                xe4 = afm.rearrange("p a b -> p (a b)")[:, par * 8192: par * 8192 + 8192]
                xe4 = xe4.rearrange("p (c d j) -> p c d j", c=8, d=DC)
                for c in range(8):
                    sXe[par].inc(gp.dma_gather(xe4[:, c, :, :], y_pool,
                                               xeidx[e][:, c * 8:(c + 1) * 8], 128, RV[128], D,
                                               transpose=True), 16)
                    gp.wait_ge(sXe[par].h, sXe[par].n)
                return sXe[par].n
            M[("g_xe", e)] = run("gpsimd", gxe)

            def gw1(gp, e=e):
                if e == 0:
                    gp.wait_ge(sT.h, M[("t_vmm", H - 1)])
                else:
                    gp.wait_ge(sT.h, M[("t_w1", e - 1, FC - 1)])
                for fc0 in range(2):
                    sW1[fc0].inc(gp.dma_start(
                        out=w1s[fc0],
                        in_=w1_d[e].rearrange("(c p) f -> p c f", p=128)[:, :, fc0 * 128:(fc0 + 1) * 128]), 16)
                M[("g_w1a", e)] = sW1[0].n
                M[("g_w1b", e)] = sW1[1].n
                return sW1[0].n
            M[("g_w1", e)] = run("gpsimd", gw1)

            for fc in range(FC):
                def tw1(te, e=e, fc=fc, par=par):
                    if fc == 0:
                        te.wait_ge(sXe[par].h, M[("g_xe", e)])
                        te.wait_ge(sW1[0].h, M[("g_w1a", e)])
                        te.wait_ge(sW1[1].h, M[("g_w1b", e)])
                        if e == 0:
                            te.wait_ge(sV.h, M[("v_ln1", NT - 1)])
                        else:
                            te.wait_ge(sA.h, M[("a_ye", e - 1, 7)])
                    elif fc >= 2:
                        te.wait_ge(sW1[fc % 2].h, M[("g_w1c", e, fc)])
                    else:
                        te.wait_ge(sW1[1].h, M[("g_w1b", e)])
                    if fc >= 2:
                        te.wait_ge(sA.h, M[("a_h", e, fc - 2)])
                    xe4 = afm.rearrange("p a b -> p (a b)")[:, par * 8192: par * 8192 + 8192]
                    xe4 = xe4.rearrange("p (c d j) -> p c d j", c=8, d=DC)
                    last = None
                    for half in range(2):
                        for dc in range(DC):
                            last = te.matmul(pA[:, (fc % 2) * 1024 + half * 512:(fc % 2) * 1024 + (half + 1) * 512],
                                             w1s[fc % 2][:, dc, :],
                                             xe4[:, 4 * half:4 * half + 4, dc, :],
                                             start=(dc == 0), stop=(dc == DC - 1))
                    return sT.inc(last)
                M[("t_w1", e, fc)] = run("tensor", tw1)

                def gw1n(gp, e=e, fc=fc):
                    if fc + 2 < FC:
                        gp.wait_ge(sT.h, M[("t_w1", e, fc)])
                        sW1[fc % 2].inc(gp.dma_start(
                            out=w1s[fc % 2],
                            in_=w1_d[e].rearrange("(c p) f -> p c f", p=128)[:, :, (fc + 2) * 128:(fc + 3) * 128]), 16)
                    return sW1[fc % 2].n
                M[("g_w1c", e, fc + 2)] = run("gpsimd", gw1n)

                def ah(sc, e=e, fc=fc):
                    sc.wait_ge(sT.h, M[("t_w1", e, fc)])
                    last = None
                    for half in range(2):
                        last = sc.activation(hbuf[:, fc, half * 512:(half + 1) * 512],
                                             pA[:, (fc % 2) * 1024 + half * 512:(fc % 2) * 1024 + (half + 1) * 512],
                                             AF.Gelu, bias=b1c[:, e, fc:fc + 1], scale=1.0)
                    return sA.inc(last)
                M[("a_h", e, fc)] = run("scalar", ah)

            for sh in range(2):
                def gw2(gp, e=e, sh=sh):
                    if not (e == 0 and sh == 0):
                        gp.wait_ge(sT.h, M[("t_w2", e if sh == 1 else e - 1, 1 - sh, FC - 1)])
                    for fc0 in range(2):
                        sW2[fc0].inc(gp.dma_start(out=w2s[fc0], in_=w2_d[e, fc0 * 128:(fc0 + 1) * 128, :]), 16)
                    M[("g_w2a", e, sh)] = sW2[0].n
                    M[("g_w2b", e, sh)] = sW2[1].n
                    return sW2[0].n
                M[("g_w2", e, sh)] = run("gpsimd", gw2)

                for fc in range(FC):
                    def tw2(te, e=e, sh=sh, fc=fc):
                        if fc == 0:
                            te.wait_ge(sW2[0].h, M[("g_w2a", e, sh)])
                            te.wait_ge(sA.h, M[("a_h", e, FC - 1)])
                            if sh == 1:
                                te.wait_ge(sA.h, M[("a_ye", e, 3)])
                            elif e == 0:
                                te.wait_ge(sV.h, M[("v_xi", E - 1)])
                                te.wait_ge(sV.h, M[("v_gtc", NT - 1)])
                            else:
                                te.wait_ge(sA.h, M[("a_ye", e - 1, 7)])
                        elif fc >= 2:
                            te.wait_ge(sW2[fc % 2].h, M[("g_w2c", e, sh, fc)])
                        else:
                            te.wait_ge(sW2[1].h, M[("g_w2b", e, sh)])
                        last = None
                        for sq in range(4):
                            stile = sh * 4 + sq
                            for dh2 in range(2):
                                last = te.matmul(bank(sq * 2 + dh2),
                                                 hbuf[:, fc, stile * 128:(stile + 1) * 128],
                                                 w2s[fc % 2][:, dh2 * 512:(dh2 + 1) * 512],
                                                 start=(fc == 0), stop=(fc == FC - 1))
                        return sT.inc(last)
                    M[("t_w2", e, sh, fc)] = run("tensor", tw2)

                    def gw2n(gp, e=e, sh=sh, fc=fc):
                        if fc + 2 < FC:
                            gp.wait_ge(sT.h, M[("t_w2", e, sh, fc)])
                            sW2[fc % 2].inc(gp.dma_start(out=w2s[fc % 2],
                                                         in_=w2_d[e, (fc + 2) * 128:(fc + 3) * 128, :]), 16)
                        return sW2[fc % 2].n
                    M[("g_w2c", e, sh, fc + 2)] = run("gpsimd", gw2n)

                for sq in range(4):
                    def aye(sc, e=e, sh=sh, sq=sq):
                        sc.wait_ge(sT.h, M[("t_w2", e, sh, FC - 1)])
                        key = ("w_ye", e, sh * 4 + sq - 2) if sq >= 2 else \
                              (("w_ye", e, sh * 4 - 2 + sq) if sh == 1 else
                               (("w_ye", e - 1, 6 + sq) if e >= 1 else None))
                        if key is not None and key in M:
                            sc.wait_ge(sYe[sq % 2].h, M[key])
                        last = None
                        for dh2 in range(2):
                            last = sc.activation(yeb[sq % 2][:, dh2 * 512:(dh2 + 1) * 512],
                                                 bank(sq * 2 + dh2), AF.Copy, bias=0.0, scale=1.0)
                        return sA.inc(last)
                    M[("a_ye", e, sh * 4 + sq)] = run("scalar", aye)

                    def wye(sync, e=e, sh=sh, sq=sq):
                        sync.wait_ge(sA.h, M[("a_ye", e, sh * 4 + sq)])
                        stile = sh * 4 + sq
                        return sYe[sq % 2].inc(sync.dma_start(
                            out=ye_pool[e * CAP + stile * 128: e * CAP + (stile + 1) * 128, :],
                            in_=yeb[sq % 2]), 16)
                    M[("w_ye", e, sh * 4 + sq)] = run("sync", wye)

        def gg1(gp):
            gp.wait_ge(sYe[0].h, sYe[0].n)
            gp.wait_ge(sYe[1].h, sYe[1].n)
            gp.wait_ge(sGi.h, M["w_gidx"])
            gp.wait_ge(sYfm.h, M["g_yfm"])
            for half in range(2):
                sYfm.inc(gp.dma_gather(g1buf[:, half * 8:(half + 1) * 8, :], ye_pool,
                                       gidx1[:, half * 64:(half + 1) * 64], T // 2, RV[1024], D), 16)
                gp.wait_ge(sYfm.h, sYfm.n)
            for half in range(2):
                sYfm.inc(gp.dma_gather(g2buf[:, half * 8:(half + 1) * 8, :], ye_pool,
                                       gidx2[:, half * 64:(half + 1) * 64], T // 2, RV[1024], D), 16)
                gp.wait_ge(sYfm.h, sYfm.n)
            return sYfm.n
        M["g_G"] = run("gpsimd", gg1)

        for lt in range(NT):
            par = lt % 2

            def tb2(te, lt=lt):
                if lt == 0:
                    te.wait_ge(sA.h, M[("a_ye", E - 1, 7)])
                if lt >= 2:
                    te.wait_ge(sV.h, M[("v_var2", lt - 2)])
                last = None
                for dh2 in range(2):
                    last = te.matmul(sbank((lt % 2) * 2 + dh2)[:, :],
                                     gttT[0:8, lt * 128:(lt + 1) * 128],
                                     b2bf[:, dh2 * 512:(dh2 + 1) * 512],
                                     start=True, stop=True)
                return sT.inc(last)
            M[("t_b2", lt)] = run("tensor", tb2)

            def vffn(v, lt=lt):
                v.wait_ge(sT.h, M[("t_b2", lt)])
                if lt == 0:
                    v.wait_ge(sYfm.h, M["g_G"])
                for dh2 in range(2):
                    sl = slice(dh2 * 512, (dh2 + 1) * 512)
                    v.scalar_tensor_tensor(ffns[:, sl], g1buf[:, lt, sl], r_w1[:, lt:lt + 1],
                                           sbank((lt % 2) * 2 + dh2)[:, :], OP.mult, OP.add)
                v.drain()
                for dh2 in range(2):
                    sl = slice(dh2 * 512, (dh2 + 1) * 512)
                    v.scalar_tensor_tensor(ffns[:, sl], g2buf[:, lt, sl], r_w2[:, lt:lt + 1],
                                           ffns[:, sl], OP.mult, OP.add)
                v.drain()
                for dh2 in range(2):
                    sl = slice(dh2 * 512, (dh2 + 1) * 512)
                    v.tensor_tensor(v1t[:, sl], ffns[:, sl], ytm[:, lt, sl], OP.add)
                v.drain()
                for dh2 in range(2):
                    sl = slice(dh2 * 512, (dh2 + 1) * 512)
                    v.scalar_tensor_tensor(sqscr[:, sl], v1t[:, sl], 1.0, v1t[:, sl], OP.mult, OP.mult)
                v.reduce_sum(ssum_c, v1t, axis=AX.X)
                v.drain()
                v.reduce_sum(ssq_c, sqscr, axis=AX.X)
                v.drain()
                v.tensor_scalar(mean_c, ssum_c, 1.0 / D, None, OP.mult)
                v.tensor_scalar(c1d, ssq_c, 1.0 / D, None, OP.mult)
                v.drain()
                return sV.inc(v.tensor_scalar(c1c, mean_c, mean_c, -1.0, OP.mult, OP.mult))
            M[("v_var2", lt)] = run("vector", vffn)

            def aln2(sc, lt=lt):
                sc.wait_ge(sV.h, M[("v_var2", lt)])
                sc.activation(rstd_c, c1d, AF.Ln, bias=c1c, scale=1.0)
                sc.drain()
                return sA.inc(sc.activation(rstd_c, rstd_c, AF.Exp, bias=0.0, scale=-0.5))
            M[("a_ln2", lt)] = run("scalar", aln2)

            def vout(v, lt=lt, par=par):
                v.wait_ge(sA.h, M[("a_ln2", lt)])
                if lt >= 2:
                    v.wait_ge(sOut[par].h, M[("w_out", lt - 2)])
                sV.inc(v.tensor_scalar(outstg[par], v1t, mean_c, rstd_c, OP.subtract, OP.mult))
                M[("v_out", lt)] = sV.n
                return sV.n
            run("vector", vout)

            def wout(sync, lt=lt, par=par):
                sync.wait_ge(sV.h, M[("v_out", lt)])
                sOut[par].inc(sync.dma_start(out=out_d[lt * 128:(lt + 1) * 128, :], in_=outstg[par]), 16)
                M[("w_out", lt)] = sOut[par].n
                return sOut[par].n
            run("sync", wout)

        def wfin(sync):
            sync.wait_ge(sOut[0].h, sOut[0].n)
            sync.wait_ge(sOut[1].h, sOut[1].n)
            return None
        run("sync", wfin)

    mybir.codegen_inst_isa_subclasses(nc)
    return nc


_NC_CACHE = {}


def kernel(**inputs):
    x = np.ascontiguousarray(np.asarray(inputs["L0_attn_x"], np.float32))
    wqkv = np.asarray(inputs["L0_attn_w_qkv"], np.float32)
    bq = np.asarray(inputs["L0_attn_b_qkv"], np.float32)
    qn_w = np.asarray(inputs["L0_attn_qn_w"], np.float32)
    kn_w = np.asarray(inputs["L0_attn_kn_w"], np.float32)
    wproj = np.asarray(inputs["L0_attn_w_proj"], np.float32)
    bproj = np.asarray(inputs["L0_attn_b_proj"], np.float32)
    wg = np.asarray(inputs["L0_ffn_Wg"], np.float32)
    w1 = np.ascontiguousarray(np.asarray(inputs["L0_ffn_W1"], np.float32))
    b1 = np.asarray(inputs["L0_ffn_b1"], np.float32)
    w2 = np.ascontiguousarray(np.asarray(inputs["L0_ffn_W2"], np.float32))
    b2 = np.asarray(inputs["L0_ffn_b2"], np.float32)

    if "nc" not in _NC_CACHE:
        _NC_CACHE["nc"] = build_nc()
    nc = _NC_CACHE["nc"]

    shared = {
        "wqkvT": np.ascontiguousarray(wqkv.T),
        "bqkv_cols": np.ascontiguousarray(bq.reshape(3, H, 64).transpose(2, 0, 1).reshape(64, 3 * H)),
        "qn_col": np.ascontiguousarray(qn_w.reshape(64, 1)),
        "kn_col": np.ascontiguousarray(kn_w.reshape(64, 1)),
        "wpT": np.ascontiguousarray(wproj.T),
        "bp_row": np.ascontiguousarray(bproj.reshape(1, D)),
        "wg": np.ascontiguousarray(wg),
        "w1": w1,
        "b1_cols": np.ascontiguousarray(b1.reshape(E, FC, 128).transpose(0, 2, 1)),
        "w2": w2,
        "b2": np.ascontiguousarray(b2),
    }
    in_maps = [dict(shared, x=np.ascontiguousarray(x[b])) for b in range(8)]
    res = run_bass_kernel_spmd(nc, in_maps, core_ids=list(range(8)))
    return np.stack([r["out"] for r in res.results]).astype(np.float32)


if __name__ == "__main__":
    nc = build_nc()
    print("built ok, instructions:", nc.n_instructions())
